# version 03
# baseline (speedup 1.0000x reference)
import contextlib
import numpy as np
import ml_dtypes
import concourse.bass as bass
import concourse.mybir as mybir
from concourse.bass_utils import run_bass_kernel_spmd

F32 = mybir.dt.float32
BF16 = mybir.dt.bfloat16
AF = mybir.ActivationFunctionType
ALU = mybir.AluOpType
AX = mybir.AxisListType

NCORES = 8
B = 2
T = 2048
D = 2048
H = 16
DK = DV = 128
CONV = 6144
NQC = 48
KCV = 4
II = 8192
C = 128
TILE = 512
NTPB = T // TILE
NCPT = TILE // C
NDC = D // 128
NIS = 16
LN_EPS = 1e-5
L2_EPS = 1e-6
NEGHALF_LN128 = float(np.float32(-0.5 * np.log(128.0)))
CONSTS_LIST = [1.0, 0.0, -1.0, -0.5, 1e-6, NEGHALF_LN128, 0.0078125, 1e-5,
               0.00048828125]
RMS_EPS = 1e-6
NEG_BIG = -1e30

DEBUG = False
PARANOID = False

DMA_QUEUES = ("qsp", "qact", "qpool")
QUEUE_ENG = {"qsp": "sp", "qact": "act", "qpool": "pool"}
COMPUTE_ENGINES = ("pe", "act", "dve", "pool")


class Plan:

    def __init__(self):
        self.items = []
        self.writer = {}
        self.readers = {}
        self.tensor_keys = {}

    @staticmethod
    def _stars(k):
        return [k[:i] + ("*",) for i in range(1, len(k))]

    def _conflicts(self, k):
        if k[-1] == "*":
            pre = k[:-1]
            out = [kk for kk in self.tensor_keys.get(k[0], ())
                   if kk[:len(pre)] == pre or
                   (kk[-1] == "*" and k[:len(kk) - 1] == kk[:-1])]
            return out
        out = [k] if k in self.tensor_keys.get(k[0], ()) else []
        for s in self._stars(k):
            if s in self.tensor_keys.get(k[0], ()):
                out.append(s)
        return out

    def _deps_for(self, reads, writes):
        deps = set()
        for k in reads:
            for kk in self._conflicts(k):
                w = self.writer.get(kk)
                if w is not None:
                    deps.add(w)
        for k in writes:
            for kk in self._conflicts(k):
                w = self.writer.get(kk)
                if w is not None:
                    deps.add(w)
                for r in self.readers.get(kk, ()):
                    deps.add(r)
        return deps

    def _commit(self, idx, reads, writes):
        for k in writes:
            if k[-1] == "*":
                pre = k[:-1]
                for kk in list(self.tensor_keys.get(k[0], ())):
                    if kk[:len(pre)] == pre and kk != k:
                        self.tensor_keys[k[0]].discard(kk)
                        self.writer.pop(kk, None)
                        self.readers.pop(kk, None)
            self.tensor_keys.setdefault(k[0], set()).add(k)
            self.writer[k] = idx
            self.readers[k] = []
        for k in reads:
            self.tensor_keys.setdefault(k[0], set()).add(k)
            self.readers.setdefault(k, []).append(idx)

    def op(self, eng, fn, reads=(), writes=(), tag="", fence=False):
        assert eng in COMPUTE_ENGINES or eng == "sp"
        reads = [tuple(r) for r in reads]
        writes = [tuple(w) for w in writes]
        idx = len(self.items)
        deps = self._deps_for(reads, writes)
        self.items.append(dict(idx=idx, eng=eng, queue=None, fn=fn, deps=deps,
                               tag=tag, inc=False, fence=fence))
        self._commit(idx, reads, writes)
        return idx

    def dma(self, queue, fn, reads=(), writes=(), tag=""):
        assert queue in DMA_QUEUES
        reads = [tuple(r) for r in reads]
        writes = [tuple(w) for w in writes]
        idx = len(self.items)
        deps = self._deps_for(reads, writes)
        self.items.append(dict(idx=idx, eng=QUEUE_ENG[queue], queue=queue,
                               fn=fn, deps=deps, tag=tag, inc=False))
        self._commit(idx, reads, writes)
        return idx

    def finalize(self, block, sems):
        counts = {q: 0 for q in DMA_QUEUES}
        for it in self.items:
            if it["queue"] is not None:
                counts[it["queue"]] += 1
                it["qnum"] = counts[it["queue"]]
        for it in self.items:
            byeng = {}
            for d in it["deps"]:
                pr = self.items[d]
                if pr["queue"] is not None:
                    key = ("q", pr["queue"])
                else:
                    if (pr["eng"] == it["eng"] and it["queue"] is None
                            and not pr.get("fence") and not PARANOID):
                        continue
                    key = ("e", pr["eng"])
                if byeng.get(key, -1) < d:
                    byeng[key] = d
            it["edeps"] = byeng
            for key, d in byeng.items():
                if key[0] == "e":
                    self.items[d]["inc"] = True
        inccnt = {e: 0 for e in COMPUTE_ENGINES + ("sp",)}
        for it in self.items:
            if it["queue"] is None and it["inc"]:
                inccnt[it["eng"]] += 1
                it["incnum"] = inccnt[it["eng"]]
        for it in self.items:
            waits = {}
            for key, d in it["edeps"].items():
                pr = self.items[d]
                if key[0] == "q":
                    k, val = pr["queue"], 16 * pr["qnum"]
                else:
                    k, val = pr["eng"], pr["incnum"]
                if waits.get(k, 0) < val:
                    waits[k] = val
            it["waits"] = waits
        for e, cnt in inccnt.items():
            assert cnt < 62000, f"sem overflow {e}: {cnt}"
        for q, cnt in counts.items():
            assert 16 * cnt < 62000, f"dma sem overflow {q}: {16 * cnt}"

        def emit_for(eng_name, eng):
            seen = {}
            for it in self.items:
                if it["eng"] != eng_name:
                    continue
                for key in sorted(it["waits"], key=str):
                    val = it["waits"][key]
                    if seen.get(key, 0) >= val:
                        continue
                    eng.wait_ge(sems[key], val)
                    seen[key] = val
                inst = it["fn"](eng)
                if it["queue"] is not None:
                    inst.then_inc(sems[it["queue"]], 16)
                elif it["inc"]:
                    inst.then_inc(sems[it["eng"]], 1)

        @block.sync
        def _(sync):
            emit_for("sp", sync)

        @block.tensor
        def _(tensor):
            emit_for("pe", tensor)

        @block.vector
        def _(vector):
            emit_for("dve", vector)

        @block.scalar
        def _(scalar):
            emit_for("act", scalar)

        @block.gpsimd
        def _(gpsimd):
            emit_for("pool", gpsimd)


def build_nc():
    nc = bass.Bass("TRN2", target_bir_lowering=False)

    x_in = nc.dram_tensor("x", [B, T, D], F32, kind="ExternalInput")
    wqkv_in = nc.dram_tensor("wqkv_blk", [NQC, 128, 2560], F32, kind="ExternalInput")
    wba_in = nc.dram_tensor("wba_blk", [128, NDC * 64], F32, kind="ExternalInput")
    wz_in = nc.dram_tensor("wz_blk", [H, 128, NDC * 128], F32, kind="ExternalInput")
    wout_in = nc.dram_tensor("wout_blk", [8, 128, H * 256], F32, kind="ExternalInput")
    wgu_in = nc.dram_tensor("wgu_blk", [NIS, 128, 16384], F32, kind="ExternalInput")
    wd_in = nc.dram_tensor("wd_blk", [4, 8, 128, 4096], F32, kind="ExternalInput")
    scal_in = nc.dram_tensor("scal_in", [64, 8], F32, kind="ExternalInput")
    idf_in = nc.dram_tensor("idf_in", [128, 128], F32, kind="ExternalInput")
    idb_in = nc.dram_tensor("idb_in", [128, 128], BF16, kind="ExternalInput")
    penb_in = nc.dram_tensor("penb_in", [128, 128], BF16, kind="ExternalInput")
    nw_in = nc.dram_tensor("nw_in", [128, 1], F32, kind="ExternalInput")
    cns_in = nc.dram_tensor("cns", [128, 16], F32, kind="ExternalInput")
    out = nc.dram_tensor("out", [B, T, D], F32, kind="ExternalOutput")

    wqkv_s = nc.dram_tensor("wqkv_s", [NQC, 128, 2560], BF16)
    wz_s = nc.dram_tensor("wz_s", [H, 128, NDC * 128], BF16)
    wout_s = nc.dram_tensor("wout_s", [8, 128, H * 256], BF16)
    wgu_s = nc.dram_tensor("wgu_s", [NIS, 128, 16384], BF16)
    wd_s = nc.dram_tensor("wd_s", [4, 8, 128, 4096], BF16)
    x2_s = nc.dram_tensor("x2_s", [B, T, D], F32)
    gflat_s = nc.dram_tensor("gflat_s", [B, NTPB, NCPT, H, C], F32)
    hfm_s = nc.dram_tensor("hfm_s", [64, 128, 512], BF16)

    p = Plan()
    dbg_outs = {}

    def dbg(name, ap_fn, reads, shape, dt):
        if not DEBUG:
            return
        t = nc.dram_tensor("dbg_" + name, shape, dt, kind="ExternalOutput")
        dbg_outs[name] = t
        p.dma("qact", lambda e: e.dma_start(out=t.ap(), in_=ap_fn()),
              reads=reads, writes=[("dbgw", name)], tag="dbg")

    def dbgdram(name, src, shape, dt):
        if not DEBUG:
            return
        t = nc.dram_tensor("dbg_" + name, shape, dt, kind="ExternalOutput")
        dbg_outs[name] = t
        p.dma("qpool", lambda e: e.dma_start(out=t.ap(), in_=src.ap()),
              reads=[("alldone",)], writes=[("dbgw", name)], tag="dbg")

    with contextlib.ExitStack() as _stk:
        qfm = _stk.enter_context(nc.sbuf_tensor("qfm", [128, H * TILE], BF16))
        kfm = _stk.enter_context(nc.sbuf_tensor("kfm", [128, H * TILE], BF16))
        vfm = _stk.enter_context(nc.sbuf_tensor("vfm", [128, H * TILE], BF16))
        xnfm = _stk.enter_context(nc.sbuf_tensor("xnfm", [128, NDC * TILE], BF16))
        bigb = _stk.enter_context(nc.sbuf_tensor("bigb", [128, 8192], BF16))
        bigc = _stk.enter_context(nc.sbuf_tensor("bigc", [128, 8192], BF16))
        wbuf = _stk.enter_context(nc.sbuf_tensor("wbuf", [128, 8192], BF16))
        woutq = _stk.enter_context(nc.sbuf_tensor("woutq", [128, 8192], BF16))
        tmpb = _stk.enter_context(nc.sbuf_tensor("tmpb", [128, 5632], BF16))
        sbf = _stk.enter_context(nc.sbuf_tensor("sbf", [128, H * DV], BF16))
        xntm = _stk.enter_context(nc.sbuf_tensor("xntm", [128, 2048], BF16))
        convs = _stk.enter_context(nc.sbuf_tensor("convs", [128, 1032], BF16))
        tails = _stk.enter_context(nc.sbuf_tensor("tails", [128, NQC * 4], BF16))
        wba = _stk.enter_context(nc.sbuf_tensor("wba", [128, NDC * 64], BF16))
        idbf = _stk.enter_context(nc.sbuf_tensor("idbf", [128, 128], BF16))
        idf32 = _stk.enter_context(nc.sbuf_tensor("idf32", [128, 128], F32))
        penin = _stk.enter_context(nc.sbuf_tensor("penin", [128, 128], BF16))
        onescol = _stk.enter_context(nc.sbuf_tensor("onescol", [128, 1], BF16))
        onesrow = _stk.enter_context(nc.sbuf_tensor("onesrow", [1, 128], BF16))
        ones16b = _stk.enter_context(nc.sbuf_tensor("ones16b", [16, 128], BF16))
        rowbb = _stk.enter_context(nc.sbuf_tensor("rowbb", [1, 512], BF16))
        xstag = _stk.enter_context(nc.sbuf_tensor("xstag", [128, 2048], F32))
        arena = _stk.enter_context(nc.sbuf_tensor("arena", [128, 3392], F32))
        sa = _stk.enter_context(nc.sbuf_tensor("sa", [128, 160], F32))
        argE = _stk.enter_context(nc.sbuf_tensor("argE", [128, 1024], F32))
        ytm = _stk.enter_context(nc.sbuf_tensor("ytm", [128, 2048], F32))
        sall = _stk.enter_context(nc.sbuf_tensor("sall", [128, H * DV], F32))
        ostg = _stk.enter_context(nc.sbuf_tensor("ostg", [128, 256], BF16))
        ms = _stk.enter_context(nc.sbuf_tensor("ms", [128, 264], F32))
        cons = _stk.enter_context(nc.sbuf_tensor("cons", [128, 16], F32))
        zs = _stk.enter_context(nc.sbuf_tensor("zs", [128, 2576], BF16))
        ogf = _stk.enter_context(nc.sbuf_tensor("ogf", [128, 512], F32))
        PF = _stk.enter_context(nc.psum_tensor("PF", [128, 3584], F32))
        PT = _stk.enter_context(nc.psum_tensor("PT", [128, 1024], BF16))
        s_pe = _stk.enter_context(nc.semaphore("s_pe"))
        s_act = _stk.enter_context(nc.semaphore("s_act"))
        s_dve = _stk.enter_context(nc.semaphore("s_dve"))
        s_pool = _stk.enter_context(nc.semaphore("s_pool"))
        s_sp = _stk.enter_context(nc.semaphore("s_sp"))
        s_qsp = _stk.enter_context(nc.semaphore("s_qsp"))
        s_qact = _stk.enter_context(nc.semaphore("s_qact"))
        s_qpool = _stk.enter_context(nc.semaphore("s_qpool"))
        block = _stk.enter_context(nc.Block())
        sems = dict(pe=s_pe, act=s_act, dve=s_dve, pool=s_pool, sp=s_sp,
                    qsp=s_qsp, qact=s_qact, qpool=s_qpool)

        CONSTS = CONSTS_LIST
        for i, v in enumerate(CONSTS):
            nc.const_aps.register(F32, float(v), cons[:, i:i + 1])
            if float(np.float32(v)) != float(v):
                nc.const_aps.register(F32, float(np.float32(v)),
                                      cons[:, i:i + 1])
        ZCOL = cons[:, 1:2]

        A_BA = 0
        A_P1 = 512
        A_TT = 1024
        A_G = 1536
        A_GC = 2048
        A_BETA = 2560
        A_Z16 = 3072
        COL_ALDT = 3200
        COL_NEA = 3201
        COL_GCL = 3202
        S_GTM, S_BAL, S_GAM, S_WAL, S_GCC = 0, 16, 32, 48, 64
        S_LNM, S_LNV, S_LNR, S_TMP, S_TMP2 = 80, 81, 82, 83, 84
        S_ST = 96
        M_NW = 0
        M_T1 = 8
        M_T2 = 136

        def pfb(bank, ncols=512, off=0):
            a = PF[:, bank * 512 + off: bank * 512 + off + ncols]
            b1 = (bank * 512 + off + ncols - 1) // 512
            return a, [("PF", bk) for bk in range(bank, b1 + 1)]

        def ptb(off, ncols):
            return PT[:, off:off + ncols], [("PT", 0)]

        def ptfence(extra_writes=()):
            p.op("pe", lambda e: e.matmul(
                PF[:, 6 * 512 + 496:6 * 512 + 512], idbf[:], idbf[:, 0:16],
                start=True, stop=True),
                reads=[("idbf", 0)],
                writes=[("PT", 0), ("PF", 6)] + list(extra_writes),
                tag="ptfence")

        CH = 128 * 2048
        CAST_JOBS = [
            ("wqkv_s", wqkv_s, wqkv_in),
            ("wba_s", None, None),
            ("wz_s", wz_s, wz_in),
            ("wout_s", wout_s, wout_in),
            ("wgu_s", wgu_s, wgu_in),
            ("wd_s", wd_s, wd_in),
        ]
        NCH = {}

        def emit_cast(region, dst, src):
            tot = int(np.prod(src.shape))
            nch = (tot + CH - 1) // CH
            NCH[region] = (nch, tot)
            for ci in range(nch):
                n = min(CH, tot - ci * CH)
                fstg = xstag if ci % 2 == 0 else ytm
                fkey = ("xstag", 0) if ci % 2 == 0 else ("ytm", 0)
                bslot = (ci // 2) % 2
                bkey = ("bigc", bslot, "cast")
                bstg = bigc[:, bslot * 2048:bslot * 2048 + 2048]

                def ld(e, src=src, ci=ci, n=n, fstg=fstg):
                    si = bass.AP(src, ci * CH, [[n // 128, 128], [1, n // 128]])
                    return e.dma_start(out=fstg[:, 0:n // 128], in_=si)
                p.dma("qsp", ld, reads=[fkey], writes=[fkey], tag="castL")

                def cp(e, n=n, fstg=fstg, bstg=bstg):
                    return e.tensor_copy(bstg[:, 0:n // 128],
                                         fstg[:, 0:n // 128])
                p.op("pool", cp, reads=[fkey], writes=[bkey], tag="castC",
                     fence=True)

                def st(e, dst=dst, ci=ci, n=n, bstg=bstg):
                    di = bass.AP(dst, ci * CH, [[n // 128, 128], [1, n // 128]])
                    return e.dma_start(out=di, in_=bstg[:, 0:n // 128])
                p.dma("qsp", st, reads=[bkey], writes=[(region, ci)],
                      tag="castS")

        emit_cast("wqkv_s", wqkv_s, wqkv_in)
        p.dma("qsp", lambda e: e.dma_start(out=xstag[:, 0:1024],
                                           in_=wba_in.ap()),
              reads=[("xstag", 0)], writes=[("xstag", 0)], tag="castL")
        p.op("pool", lambda e: e.tensor_copy(wba[:], xstag[:, 0:1024]),
             reads=[("xstag", 0)], writes=[("wba", 0)], tag="castC",
             fence=True)
        emit_cast("wz_s", wz_s, wz_in)
        emit_cast("wout_s", wout_s, wout_in)
        emit_cast("wgu_s", wgu_s, wgu_in)
        emit_cast("wd_s", wd_s, wd_in)

        def wreg(region, i, nunits):
            nch, tot = NCH[region]
            last_el = (i + 1) * (tot // nunits) - 1
            return (region, min(last_el // CH, nch - 1))

        p.dma("qact", lambda e: e.dma_start(out=cons[:], in_=cns_in.ap()),
              writes=[("cons", 0)])
        p.dma("qact", lambda e: e.dma_start(out=idbf[:], in_=idb_in.ap()),
              writes=[("idbf", 0)])
        p.dma("qact", lambda e: e.dma_start(out=idf32[:], in_=idf_in.ap()),
              writes=[("idf32", 0)])
        p.dma("qact", lambda e: e.dma_start(out=penin[:], in_=penb_in.ap()),
              writes=[("penin", 0)])
        p.dma("qact", lambda e: e.dma_start(
            out=arena[0:64, COL_ALDT:COL_ALDT + 8], in_=scal_in.ap()),
            writes=[("arena", "scal")])
        p.dma("qact", lambda e: e.dma_start(out=ms[:, 0:1], in_=nw_in.ap()),
              writes=[("ms", "nw")])
        p.op("dve", lambda e: e.memset(onescol[:], 1.0), writes=[("onescol", 0)])
        p.op("dve", lambda e: e.memset(onesrow[:], 1.0), writes=[("onesrow", 0)])
        p.op("dve", lambda e: e.memset(ones16b[:], 1.0), writes=[("ones16b", 0)])
        p.op("dve", lambda e: e.memset(arena[0:16, A_Z16:A_Z16 + 128], 0.0),
             writes=[("arena", "z16")])
        p.op("act", lambda e: e.activation(
            arena[0:16, COL_NEA:COL_NEA + 1],
            arena[0:16, COL_ALDT:COL_ALDT + 1], AF.Exp),
            reads=[("arena", "scal"), ("cons", 0)], writes=[("arena", "nea0")])
        p.op("dve", lambda e: e.tensor_single_scalar(
            arena[32:48, COL_NEA:COL_NEA + 1],
            arena[0:16, COL_NEA:COL_NEA + 1], -1.0, ALU.mult),
            reads=[("arena", "nea0")], writes=[("arena", "nea")],
            fence=True)

        def ln_rstd(src_ap, src_reads, tag):
            p.op("dve", lambda e: e.reduce_sum(
                sa[:, S_ST:S_ST + 1], src_ap[:, 0:2048], axis=AX.X),
                reads=src_reads, writes=[("sa", "sum")], tag=tag, fence=True)
            p.op("act", lambda e: e.activation(
                ytm[:], src_ap[:, 0:2048], AF.Square),
                reads=src_reads, writes=[("ytm", 0)], tag=tag)
            p.op("dve", lambda e: e.reduce_sum(
                sa[:, S_ST + 1:S_ST + 2], ytm[:], axis=AX.X),
                reads=[("ytm", 0)], writes=[("sa", "ssq")], tag=tag,
                fence=True)
            p.op("dve", lambda e: e.tensor_single_scalar(
                sa[:, S_LNM:S_LNM + 1], sa[:, S_ST:S_ST + 1], 0.00048828125,
                ALU.mult), reads=[("sa", "sum")], writes=[("sa", "mv")],
                tag=tag, fence=True)
            p.op("dve", lambda e: e.tensor_tensor(
                sa[:, S_TMP:S_TMP + 1], sa[:, S_LNM:S_LNM + 1],
                sa[:, S_LNM:S_LNM + 1], ALU.mult),
                reads=[("sa", "mv")], writes=[("sa", "msq")], tag=tag,
                fence=True)
            p.op("dve", lambda e: e.tensor_scalar(
                sa[:, S_LNV:S_LNV + 1], sa[:, S_ST + 1:S_ST + 2],
                cons[:, 8:9], sa[:, S_TMP:S_TMP + 1], ALU.mult, ALU.subtract),
                reads=[("sa", "ssq"), ("sa", "msq"), ("cons", 0)],
                writes=[("sa", "var")], tag=tag, fence=True)
            p.op("dve", lambda e: e.tensor_single_scalar(
                sa[:, S_LNV:S_LNV + 1], sa[:, S_LNV:S_LNV + 1], LN_EPS,
                ALU.add), reads=[("sa", "var")], writes=[("sa", "var")],
                tag=tag, fence=True)
            p.op("act", lambda e: e.activation(
                sa[:, S_TMP2:S_TMP2 + 1], sa[:, S_LNV:S_LNV + 1], AF.Sqrt),
                reads=[("sa", "var")], writes=[("sa", "sq")], tag=tag)
            p.op("dve", lambda e: e.reciprocal(
                sa[:, S_LNR:S_LNR + 1], sa[:, S_TMP2:S_TMP2 + 1]),
                reads=[("sa", "sq")], writes=[("sa", "r")], tag=tag,
                fence=True)

        def transpose_16_to(dstbuf, dst_is_tile, tg, extra_writes, name, tag):
            for rr in range(2):
                for j in range(8):
                    dc = rr * 8 + j
                    ap_, bk = ptb(j * 128, 128)
                    p.op("pe", lambda e, dc=dc, ap_=ap_: e.transpose(
                        ap_, xntm[:, dc * 128:(dc + 1) * 128], idbf[:]),
                        reads=[("xntm", 0), ("idbf", 0)], writes=bk, tag=tag)
                ptfence()

                def evac(e, rr=rr, tg=tg):
                    src = PT[:, 0:1024].rearrange("p (j f) -> p j f", j=8)
                    wdt = TILE if dst_is_tile else 512
                    d = dstbuf[:].rearrange("p (dc f) -> p dc f", f=wdt)[
                        :, rr * 8:(rr + 1) * 8, tg * 128:(tg + 1) * 128]
                    return e.tensor_copy(d, src)
                wr = [(name, rr * 8 + j, tg) for j in range(8)]
                if extra_writes and rr == 0:
                    wr = list(extra_writes) + wr
                p.op("dve", evac, reads=[("PT", 0)], writes=wr, tag=tag + "ev")

        def attn_tile(b, ti):
            t0 = ti * TILE
            for tg in range(4):
                p.dma("qact", lambda e, tg=tg: e.dma_start(
                    out=xstag[:],
                    in_=x_in[b, t0 + tg * 128:t0 + (tg + 1) * 128, :]),
                    reads=[("xstag", 0)], writes=[("xstag", 0)], tag="xload")
                ln_rstd(xstag, [("xstag", 0)], "ln1")
                p.op("dve", lambda e: e.tensor_scalar(
                    xntm[:], xstag[:], sa[:, S_LNM:S_LNM + 1],
                    sa[:, S_LNR:S_LNR + 1], ALU.subtract, ALU.mult),
                    reads=[("xstag", 0), ("sa", "mv"), ("sa", "r")],
                    writes=[("xntm", 0)], tag="ln1ap")
                if b == 0 and ti == 0 and tg == 0:
                    dbg("xstag0", lambda: xstag[:], [("xstag", 0)],
                        [128, 2048], F32)
                    dbg("sa0", lambda: sa[:],
                        [("sa", "mv"), ("sa", "r"), ("sa", "sum"),
                         ("sa", "ssq"), ("sa", "var"), ("sa", "msq"),
                         ("sa", "sq")],
                        [128, 160], F32)
                    dbg("xntm0", lambda: xntm[:], [("xntm", 0)],
                        [128, 2048], BF16)
                transpose_16_to(xnfm, True, tg, None, "xnfm", "xnT")

            if b == 0 and ti == 0:
                dbg("xnfm", lambda: xnfm[:],
                    [("xnfm", dc, tg) for dc in range(NDC) for tg in range(4)],
                    [128, NDC * TILE], BF16)
            for qi in range(NQC + 1):
                if qi < NQC:
                    par = qi % 2
                    p.dma("qsp", lambda e, qi=qi, par=par: e.dma_start(
                        out=wbuf[:, par * 2560:par * 2560 + 2560],
                        in_=wqkv_s[qi]),
                        reads=[("wbuf", par), wreg("wqkv_s", qi, NQC)],
                        writes=[("wbuf", par)], tag="wqkvL")
                if qi == 0:
                    continue
                q = qi - 1
                par = q % 2
                wv = wbuf[:, par * 2560:(par + 1) * 2560]
                pa, pk = pfb(q % 2)
                for dc in range(NDC):
                    p.op("pe", lambda e, dc=dc, pa=pa, wv=wv: e.matmul(
                        pa, wv[:, dc * 128:(dc + 1) * 128],
                        xnfm[:, dc * TILE:(dc + 1) * TILE],
                        start=(dc == 0), stop=(dc == NDC - 1)),
                        reads=[("wbuf", par)] +
                        [("xnfm", dc, tg) for tg in range(4)],
                        writes=pk, tag="qkvmm")
                cst = convs[:, (q % 2) * 516:(q % 2) * 516 + 516]
                ck = ("convs", q % 2)
                p.op("act", lambda e, pa=pa, cst=cst: e.activation(
                    cst[:, 4:516], pa, AF.Copy), reads=pk, writes=[ck],
                    tag="qkvev")
                p.op("dve", lambda e, q=q, cst=cst: e.tensor_copy(
                    cst[:, 1:4], tails[:, q * 4:q * 4 + 3]),
                    reads=[("tails", q)], writes=[ck], tag="tailin")
                pc, pck = pfb(2 + q % 2)
                for j in range(KCV):
                    p.op("pe", lambda e, j=j, wv=wv, pc=pc, cst=cst: e.matmul(
                        pc, wv[:, 2048 + j * 128:2048 + (j + 1) * 128],
                        cst[:, 1 + j:513 + j],
                        start=(j == 0), stop=(j == KCV - 1)),
                        reads=[("wbuf", par), ck], writes=pck, tag="convmm")
                p.op("dve", lambda e, q=q, cst=cst: e.tensor_copy(
                    tails[:, q * 4:q * 4 + 3], cst[:, 513:516]),
                    reads=[ck], writes=[("tails", q)], tag="tailout")
                if q < 16:
                    dst, name, hh = qfm, "qfm", q
                elif q < 32:
                    dst, name, hh = kfm, "kfm", q - 16
                else:
                    dst, name, hh = vfm, "vfm", q - 32
                p.op("act", lambda e, pc=pc, dst=dst, hh=hh: e.activation(
                    dst[:, hh * TILE:(hh + 1) * TILE], pc, AF.Silu),
                    reads=pck,
                    writes=[(name, hh, cl) for cl in range(NCPT)], tag="silu")

            if b == 0 and ti == 0:
                dbg("qraw", lambda: qfm[:],
                    [("qfm", h, cl) for h in range(H) for cl in range(4)],
                    [128, H * TILE], BF16)
                dbg("kraw", lambda: kfm[:],
                    [("kfm", h, cl) for h in range(H) for cl in range(4)],
                    [128, H * TILE], BF16)
                dbg("vraw", lambda: vfm[:],
                    [("vfm", h, cl) for h in range(H) for cl in range(4)],
                    [128, H * TILE], BF16)
            for name, buf, qbias in (("qfm", qfm, True), ("kfm", kfm, False)):
                for h in range(H):
                    src = buf[:, h * TILE:(h + 1) * TILE]
                    skeys = [(name, h, cl) for cl in range(NCPT)]
                    p.op("act", lambda e, src=src: e.activation(
                        zs[:, 0:512], src, AF.Square),
                        reads=skeys, writes=[("zs", "sq")], tag="l2sq")
                    pa, pk = pfb(6, 512)
                    p.op("pe", lambda e, pa=pa: e.matmul(
                        pa[0:1, :], onescol[:], zs[:, 0:512],
                        start=True, stop=True),
                        reads=[("zs", "sq"), ("onescol", 0)], writes=pk,
                        tag="l2mm")
                    p.op("act", lambda e, pa=pa: e.activation(
                        rowbb[0:1, :], pa[0:1, :], AF.Ln, bias=L2_EPS),
                        reads=pk, writes=[("rowbb", 0)], tag="l2ln")
                    p.op("act", lambda e, qbias=qbias: e.activation(
                        rowbb[0:1, :], rowbb[0:1, :], AF.Exp, scale=-0.5,
                        bias=(NEGHALF_LN128 if qbias else 0.0)),
                        reads=[("rowbb", 0)], writes=[("rowbb", 0)], tag="l2ex")
                    pb, pbk = pfb(6, 512)
                    p.op("pe", lambda e, pb=pb: e.matmul(
                        pb, onesrow[:], rowbb[0:1, :], start=True, stop=True),
                        reads=[("rowbb", 0), ("onesrow", 0)], writes=pbk,
                        tag="l2bc")
                    p.op("dve", lambda e, src=src, pb=pb: e.tensor_tensor(
                        src, src, pb, ALU.mult),
                        reads=skeys + pbk, writes=skeys, tag="l2mul")

            if b == 0 and ti == 0:
                dbg("qn", lambda: qfm[:],
                    [("qfm", h, cl) for h in range(H) for cl in range(4)],
                    [128, H * TILE], BF16)
                dbg("kn", lambda: kfm[:],
                    [("kfm", h, cl) for h in range(H) for cl in range(4)],
                    [128, H * TILE], BF16)
            pa, pk = pfb(4, 512)
            for dc in range(NDC):
                p.op("pe", lambda e, dc=dc, pa=pa: e.matmul(
                    pa[0:64, :], wba[:, dc * 64:(dc + 1) * 64],
                    xnfm[:, dc * TILE:(dc + 1) * TILE],
                    start=(dc == 0), stop=(dc == NDC - 1)),
                    reads=[("wba", 0)] + [("xnfm", dc, tg) for tg in range(4)],
                    writes=pk, tag="bamm")
            p.op("act", lambda e, pa=pa: e.activation(
                arena[0:64, A_BA:A_BA + 512], pa[0:64, :], AF.Copy),
                reads=pk, writes=[("arena", "ba")], tag="baev")
            p.op("act", lambda e: e.activation(
                arena[0:16, A_BETA:A_BETA + 512],
                arena[0:16, A_BA:A_BA + 512], AF.Sigmoid),
                reads=[("arena", "ba")], writes=[("arena", "beta")], tag="beta")
            p.op("dve", lambda e: e.tensor_scalar(
                arena[32:48, A_BA:A_BA + 512], arena[32:48, A_BA:A_BA + 512],
                arena[32:48, COL_ALDT:COL_ALDT + 1], ZCOL[32:48, :],
                ALU.add, ALU.add),
                reads=[("arena", "ba"), ("arena", "scal"), ("cons", 0)],
                writes=[("arena", "ba")], tag="xb")
            p.op("act", lambda e: e.activation(
                arena[32:48, A_P1:A_P1 + 512], arena[32:48, A_BA:A_BA + 512],
                AF.Relu), reads=[("arena", "ba")], writes=[("arena", "p1")],
                tag="sp1")
            p.op("act", lambda e: e.activation(
                arena[32:48, A_TT:A_TT + 512], arena[32:48, A_BA:A_BA + 512],
                AF.Abs), reads=[("arena", "ba")], writes=[("arena", "tt")],
                tag="sp2")
            p.op("act", lambda e: e.activation(
                arena[32:48, A_TT:A_TT + 512], arena[32:48, A_TT:A_TT + 512],
                AF.Exp, scale=-1.0),
                reads=[("arena", "tt")], writes=[("arena", "tt")], tag="sp3")
            p.op("act", lambda e: e.activation(
                arena[32:48, A_TT:A_TT + 512], arena[32:48, A_TT:A_TT + 512],
                AF.Ln, bias=1.0),
                reads=[("arena", "tt")], writes=[("arena", "tt")], tag="sp4")
            p.op("dve", lambda e: e.tensor_tensor(
                arena[32:48, A_TT:A_TT + 512], arena[32:48, A_P1:A_P1 + 512],
                arena[32:48, A_TT:A_TT + 512], ALU.add),
                reads=[("arena", "p1"), ("arena", "tt")],
                writes=[("arena", "tt")], tag="sp5")
            p.op("dve", lambda e: e.tensor_scalar(
                arena[0:16, A_G:A_G + 512], arena[32:48, A_TT:A_TT + 512],
                arena[32:48, COL_NEA:COL_NEA + 1], ZCOL[32:48, :],
                ALU.mult, ALU.add),
                reads=[("arena", "tt"), ("arena", "nea"), ("cons", 0)],
                writes=[("arena", "g")], tag="gfin")
            for cl in range(NCPT):
                p.op("dve", lambda e, cl=cl: e.tensor_tensor_scan(
                    arena[0:16, A_GC + cl * 128:A_GC + (cl + 1) * 128],
                    arena[0:16, A_G + cl * 128:A_G + (cl + 1) * 128],
                    arena[0:16, A_Z16:A_Z16 + 128], 0.0, ALU.add, ALU.add),
                    reads=[("arena", "g"), ("arena", "z16")],
                    writes=[("arena", "G", cl)], tag="cumsum", fence=True)

            def gfw(e, b=b, ti=ti):
                src = arena[0:16, A_GC:A_GC + 512].rearrange(
                    "h (c f) -> h c f", f=128)
                dst = gflat_s[b, ti].rearrange("c h f -> h c f")
                return e.dma_start(out=dst, in_=src)
            p.dma("qact", gfw,
                  reads=[("arena", "G", cl) for cl in range(NCPT)],
                  writes=[("gflat", b, ti)], tag="gflatw")

            if b == 0 and ti == 0:
                dbg("arena", lambda: arena[:],
                    [("arena", "g"), ("arena", "beta")] +
                    [("arena", "G", cl) for cl in range(NCPT)],
                    [128, 3392], F32)
            for cl in range(NCPT):
                mixer_chunk(b, ti, cl)
                if DEBUG and b == 0 and ti == 0 and cl == 0:
                    dbg("y0", lambda: bigc[:],
                        [("bigc", 3, "y", h) for h in range(H)] +
                        [("bigc", 1, "upr", h) for h in range(H)],
                        [128, 8192], BF16)
                    dbg("sall0", lambda: sall[:],
                        [("sall", h) for h in range(H)],
                        [128, H * DV], F32)
                    dbg("eswork", lambda: argE[:], [("argE", j) for j in range(8)],
                        [128, 1024], F32)
            if b == 0 and ti == 0:
                dbg("oall", lambda: vfm[:],
                    [("vfm", h, cl) for h in range(H) for cl in range(4)],
                    [128, H * TILE], BF16)
            z_stage(b, ti)
            if b == 0 and ti == 0:
                dbg("onorm", lambda: vfm[:],
                    [("vfm", h, cl) for h in range(H) for cl in range(4)],
                    [128, H * TILE], BF16)
            wout_stage(b, ti)

        def mixer_chunk(b, ti, cl):
            def gbc(e, b=b, ti=ti, cl=cl):
                off = (((b * NTPB + ti) * NCPT) + cl) * H * C
                src = bass.AP(gflat_s, off, [[0, 128], [1, 2048]])
                return e.dma_start(out=ytm[:], in_=src)
            p.dma("qact", gbc, reads=[("gflat", b, ti), ("ytm", 0)],
                  writes=[("ytm", 0)], tag="gbc")

            pa, pk = pfb(6, 16)
            p.op("pe", lambda e, cl=cl, pa=pa: e.transpose(
                pa, arena[0:16, A_GC + cl * 128:A_GC + (cl + 1) * 128],
                idf32[0:16, 0:16]),
                reads=[("arena", "G", cl), ("idf32", 0)], writes=pk, tag="gT")
            ptfence(extra_writes=pk)
            p.op("act", lambda e, pa=pa: e.activation(
                sa[:, S_GTM:S_GTM + 16], pa, AF.Copy),
                reads=pk, writes=[("sa", "gtm")], tag="gTev")
            p.op("act", lambda e: e.activation(
                sa[:, S_GAM:S_GAM + 16], sa[:, S_GTM:S_GTM + 16], AF.Exp),
                reads=[("sa", "gtm")], writes=[("sa", "gam")], tag="gam")
            pb, pbk = pfb(6, 16, off=16)
            p.op("pe", lambda e, cl=cl, pb=pb: e.transpose(
                pb, arena[0:16, A_BETA + cl * 128:A_BETA + (cl + 1) * 128],
                idf32[0:16, 0:16]),
                reads=[("arena", "beta"), ("idf32", 0)], writes=pbk, tag="bT")
            ptfence(extra_writes=pbk)
            p.op("act", lambda e, pb=pb: e.activation(
                sa[:, S_BAL:S_BAL + 16], pb, AF.Copy),
                reads=pbk, writes=[("sa", "bal")], tag="bTev")
            p.op("dve", lambda e, cl=cl: e.tensor_scalar(
                arena[0:16, A_TT:A_TT + 128],
                arena[0:16, A_GC + cl * 128:A_GC + (cl + 1) * 128],
                arena[0:16, A_GC + cl * 128 + 127:A_GC + cl * 128 + 128],
                ZCOL[0:16, :], ALU.subtract, ALU.add),
                reads=[("arena", "G", cl), ("cons", 0)],
                writes=[("arena", "tt")], tag="wrow")
            p.op("act", lambda e: e.activation(
                arena[0:16, A_TT:A_TT + 128], arena[0:16, A_TT:A_TT + 128],
                AF.Exp, scale=-1.0),
                reads=[("arena", "tt")], writes=[("arena", "tt")], tag="wrex")
            pw, pwk = pfb(6, 16, off=32)
            p.op("pe", lambda e, pw=pw: e.transpose(
                pw, arena[0:16, A_TT:A_TT + 128], idf32[0:16, 0:16]),
                reads=[("arena", "tt"), ("idf32", 0)], writes=pwk, tag="wT")
            ptfence(extra_writes=pwk)
            p.op("act", lambda e, pw=pw: e.activation(
                sa[:, S_WAL:S_WAL + 16], pw, AF.Copy),
                reads=pwk, writes=[("sa", "wal")], tag="wTev")
            p.op("act", lambda e, cl=cl: e.activation(
                arena[0:16, COL_GCL:COL_GCL + 1],
                arena[0:16, A_GC + cl * 128 + 127:A_GC + cl * 128 + 128],
                AF.Exp),
                reads=[("arena", "G", cl)], writes=[("arena", "gcl")],
                tag="gcl")
            p.op("dve", lambda e: e.tensor_scalar(
                zs[0:16, 2048:2064], idbf[0:16, 0:16],
                arena[0:16, COL_GCL:COL_GCL + 1], ZCOL[0:16, :],
                ALU.mult, ALU.add),
                reads=[("arena", "gcl"), ("idbf", 0), ("cons", 0)],
                writes=[("zs", "gdiag")], tag="gdiag")
            pg, pgk = pfb(6, 16, off=48)
            p.op("pe", lambda e, pg=pg: e.matmul(
                pg, ones16b[:], zs[0:16, 2048:2064], start=True, stop=True),
                reads=[("zs", "gdiag"), ("ones16b", 0)], writes=pgk,
                tag="gccmm")
            p.op("act", lambda e, pg=pg: e.activation(
                sa[:, S_GCC:S_GCC + 16], pg, AF.Copy),
                reads=pgk, writes=[("sa", "gcc")], tag="gccev")

            for half in range(2):
                for j in range(8):
                    h = half * 8 + j
                    p.op("dve", lambda e, h=h, j=j: e.tensor_scalar(
                        argE[:, j * 128:(j + 1) * 128],
                        ytm[:, h * 128:(h + 1) * 128],
                        sa[:, S_GTM + h:S_GTM + h + 1], ZCOL,
                        ALU.subtract, ALU.min),
                        reads=[("ytm", 0), ("sa", "gtm"), ("cons", 0)],
                        writes=[("argE", j)], tag="arg")
                    p.op("dve", lambda e, j=j: e.tensor_tensor(
                        argE[:, j * 128:(j + 1) * 128],
                        argE[:, j * 128:(j + 1) * 128], penin[:], ALU.add),
                        reads=[("argE", j), ("penin", 0)],
                        writes=[("argE", j)], tag="argpen")
                wr = [("bigc", 1, "*")] if half == 0 else []
                p.op("act", lambda e, half=half: e.activation(
                    bigc[:, 2048 + half * 1024:2048 + (half + 1) * 1024],
                    argE[:], AF.Exp),
                    reads=[("argE", j) for j in range(8)],
                    writes=wr + [("bigc", 1, "eim", half)], tag="eim")
            for h in range(H):
                wr = [("bigc", 0, "*")] if h == 0 else []
                p.op("dve", lambda e, h=h: e.tensor_tensor(
                    bigc[:, h * 128:(h + 1) * 128],
                    bigc[:, 2048 + h * 128:2048 + (h + 1) * 128],
                    idbf[:], ALU.subtract),
                    reads=[("bigc", 1, "eim", h // 8), ("idbf", 0)],
                    writes=wr + [("bigc", 0, "es", h)], tag="es")
                p.op("dve", lambda e, h=h: e.tensor_scalar(
                    bigc[:, h * 128:(h + 1) * 128],
                    bigc[:, h * 128:(h + 1) * 128],
                    sa[:, S_BAL + h:S_BAL + h + 1], ZCOL, ALU.mult, ALU.add),
                    reads=[("bigc", 0, "es", h), ("sa", "bal"), ("cons", 0)],
                    writes=[("bigc", 0, "es", h)], tag="esb")
            for h in range(H):
                pa2, pk2 = pfb(h // 4, 128, off=(h % 4) * 128)
                ksl = kfm[:, h * TILE + cl * 128:h * TILE + (cl + 1) * 128]
                p.op("pe", lambda e, pa2=pa2, ksl=ksl: e.matmul(
                    pa2, ksl, ksl, start=True, stop=True),
                    reads=[("kfm", h, cl)], writes=pk2, tag="kk")
            p.op("dve", lambda e: e.tensor_tensor(
                bigc[:, 4096:6144], PF[:, 0:2048], bigc[:, 0:2048], ALU.mult),
                reads=[("PF", bk) for bk in range(4)] +
                [("bigc", 0, "es", h) for h in range(H)],
                writes=[("bigc", 2, "*"), ("bigc", 2, "abf")], tag="abf")
            for h in range(H):
                pa2, pk2 = pfb(h // 4, 128, off=(h % 4) * 128)
                ksl = kfm[:, h * TILE + cl * 128:h * TILE + (cl + 1) * 128]
                qsl = qfm[:, h * TILE + cl * 128:h * TILE + (cl + 1) * 128]
                p.op("pe", lambda e, pa2=pa2, ksl=ksl, qsl=qsl: e.matmul(
                    pa2, ksl, qsl, start=True, stop=True),
                    reads=[("kfm", h, cl), ("qfm", h, cl),
                           ("bigc", 2, "abf")],
                    writes=pk2, tag="kq")
            p.op("dve", lambda e: e.tensor_tensor(
                bigc[:, 0:2048], PF[:, 0:2048], bigc[:, 2048:4096], ALU.mult),
                reads=[("PF", bk) for bk in range(4)] +
                [("bigc", 1, "eim", 0), ("bigc", 1, "eim", 1)] +
                [("bigc", 0, "es", h) for h in range(H)],
                writes=[("bigc", 0, "es", h) for h in range(H)], tag="om")

            for half in range(2):
                for j in range(8):
                    h = half * 8 + j
                    vsl = vfm[:, h * TILE + cl * 128:h * TILE + (cl + 1) * 128]
                    pta, ptk = ptb(j * 128, 128)
                    p.op("pe", lambda e, pta=pta, vsl=vsl: e.transpose(
                        pta, vsl, idbf[:]),
                        reads=[("vfm", h, cl), ("idbf", 0)], writes=ptk,
                        tag="vT")
                ptfence()
                for j in range(8):
                    h = half * 8 + j
                    ksl = kfm[:, h * TILE + cl * 128:h * TILE + (cl + 1) * 128]
                    pta, ptk = ptb(j * 128, 128)
                    pa2, pk2 = pfb(4 + j // 4, 128, off=(j % 4) * 128)
                    p.op("pe", lambda e, pa2=pa2, ksl=ksl, h=h: e.matmul(
                        pa2, ksl, sbf[:, h * DV:(h + 1) * DV],
                        start=True, stop=True),
                        reads=[("kfm", h, cl), ("sbf", h)], writes=pk2,
                        tag="ks0")
                    p.op("dve", lambda e, pa2=pa2, h=h: e.tensor_scalar(
                        ms[:, M_T1:M_T1 + 128], pa2,
                        sa[:, S_GAM + h:S_GAM + h + 1], ZCOL,
                        ALU.mult, ALU.add),
                        reads=pk2 + [("sa", "gam"), ("cons", 0)],
                        writes=[("ms", "t1")], tag="t1")
                    wr = [("bigc", 3, "*")] if h == 0 else []
                    p.op("dve", lambda e, pta=pta, h=h: e.tensor_tensor(
                        bigc[:, 6144 + h * 128:6144 + (h + 1) * 128],
                        pta, ms[:, M_T1:M_T1 + 128], ALU.subtract),
                        reads=ptk + [("ms", "t1")],
                        writes=wr + [("bigc", 3, "y", h)], tag="r0")

            def apply_level(lhs, sub, tagl):
                for half in range(2):
                    pa2, pk2 = pfb(half * 2, 1024)
                    for j in range(8):
                        h = half * 8 + j
                        la, lr = lhs(h)
                        p.op("pe", lambda e, la=la, h=h, half=half, j=j: e.matmul(
                            PF[:, half * 1024 + j * 128:
                               half * 1024 + (j + 1) * 128],
                            la, bigc[:, 6144 + h * 128:6144 + (h + 1) * 128],
                            start=True, stop=True),
                            reads=lr + [("bigc", 3, "y", h)], writes=pk2,
                            tag="apl" + tagl)
                    op = ALU.subtract if sub else ALU.add
                    ys = [("bigc", 3, "y", half * 8 + j) for j in range(8)]
                    p.op("dve", lambda e, half=half, pa2=pa2, op=op:
                         e.tensor_tensor(
                             bigc[:, 6144 + half * 1024:
                                  6144 + (half + 1) * 1024],
                             bigc[:, 6144 + half * 1024:
                                  6144 + (half + 1) * 1024],
                             pa2, op),
                         reads=pk2 + ys, writes=ys, tag="yup" + tagl)

            for half in range(2):
                for j in range(8):
                    h = half * 8 + j
                    pta, ptk = ptb(j * 128, 128)
                    p.op("pe", lambda e, pta=pta, h=h: e.transpose(
                        pta, bigc[:, 4096 + h * 128:4096 + (h + 1) * 128],
                        idbf[:]),
                        reads=[("bigc", 2, "abf"), ("idbf", 0)], writes=ptk,
                        tag="ptT")
                ptfence()
                wr = [("bigb", 0, "*")] if half == 0 else []
                p.op("act", lambda e, half=half: e.activation(
                    bigb[:, half * 1024:(half + 1) * 1024], PT[:, 0:1024],
                    AF.Copy), reads=[("PT", 0)],
                    writes=wr + [("bigb", 0, "pw", 1, half)], tag="ptev")

            apply_level(lambda h: (
                bigc[:, 4096 + h * 128:4096 + (h + 1) * 128],
                [("bigc", 2, "abf")]), True, "0")

            SLOT_A = (bigc, 4096, ("bigc", 2))
            SLOT_0 = (bigb, 0, ("bigb", 0))
            SLOT_1 = (bigb, 2048, ("bigb", 1))
            SLOT_2 = (bigb, 4096, ("bigb", 2))
            cur_p = (SLOT_A, [("bigc", 2, "abf")])
            cur_pt = (SLOT_0, [("bigb", 0, "pw", 1, hf) for hf in range(2)])
            free = [SLOT_1, SLOT_2]
            for lvl in (2, 4, 8, 16, 32, 64):
                dst = free.pop(0)
                dkeys = []
                for half in range(2):
                    pa2, pk2 = pfb(4, 1024)
                    for j in range(8):
                        h = half * 8 + j
                        p.op("pe", lambda e, h=h, j=j, pc=cur_p, pt_=cur_pt:
                             e.matmul(
                                 PF[:, 2048 + j * 128:2048 + (j + 1) * 128],
                                 pt_[0][0][:, pt_[0][1] + h * 128:
                                           pt_[0][1] + (h + 1) * 128],
                                 pc[0][0][:, pc[0][1] + h * 128:
                                          pc[0][1] + (h + 1) * 128],
                                 start=True, stop=True),
                             reads=cur_pt[1] + cur_p[1], writes=pk2,
                             tag=f"sq{lvl}")
                    wr = [dst[2] + ("*",)] if half == 0 else []
                    k = dst[2] + ("pw", lvl, half)
                    dkeys.append(k)
                    p.op("act", lambda e, half=half, dst=dst: e.activation(
                        dst[0][:, dst[1] + half * 1024:
                               dst[1] + (half + 1) * 1024],
                        PF[:, 2048:3072], AF.Copy),
                        reads=pk2, writes=wr + [k], tag="sqev")
                new_p = ((dst[0], dst[1], dst[2]), dkeys)
                if lvl != 64:
                    dst2 = free.pop(0)
                    dkeys2 = []
                    for half in range(2):
                        pa2, pk2 = pfb(4, 1024)
                        for j in range(8):
                            h = half * 8 + j
                            p.op("pe", lambda e, h=h, j=j, pc=cur_p,
                                 pt_=cur_pt: e.matmul(
                                     PF[:, 2048 + j * 128:
                                        2048 + (j + 1) * 128],
                                     pc[0][0][:, pc[0][1] + h * 128:
                                              pc[0][1] + (h + 1) * 128],
                                     pt_[0][0][:, pt_[0][1] + h * 128:
                                               pt_[0][1] + (h + 1) * 128],
                                     start=True, stop=True),
                                 reads=cur_pt[1] + cur_p[1], writes=pk2,
                                 tag=f"sqt{lvl}")
                        wr = [dst2[2] + ("*",)] if half == 0 else []
                        k = dst2[2] + ("pwt", lvl, half)
                        dkeys2.append(k)
                        p.op("act", lambda e, half=half, dst2=dst2:
                             e.activation(
                                 dst2[0][:, dst2[1] + half * 1024:
                                        dst2[1] + (half + 1) * 1024],
                                 PF[:, 2048:3072], AF.Copy),
                             reads=pk2, writes=wr + [k], tag="sqtev")
                    new_pt = ((dst2[0], dst2[1], dst2[2]), dkeys2)
                else:
                    new_pt = None
                free.append(cur_p[0])
                if cur_pt[0] not in free and (new_pt is None or
                                              cur_pt[0] is not new_pt[0]):
                    free.append(cur_pt[0])
                cur_p = ((new_p[0][0], new_p[0][1], new_p[0][2]), new_p[1])
                cur_pt = new_pt if new_pt is not None else cur_pt

                def mk_lhs(np_=cur_p):
                    def f(h):
                        sl = np_[0]
                        return (sl[0][:, sl[1] + h * 128:sl[1] + (h + 1) * 128],
                                np_[1])
                    return f
                apply_level(mk_lhs(), False, str(lvl))

            for h in range(H):
                p.op("dve", lambda e, h=h: e.tensor_scalar(
                    bigc[:, 6144 + h * 128:6144 + (h + 1) * 128],
                    bigc[:, 6144 + h * 128:6144 + (h + 1) * 128],
                    sa[:, S_BAL + h:S_BAL + h + 1], ZCOL, ALU.mult, ALU.add),
                    reads=[("bigc", 3, "y", h), ("sa", "bal"), ("cons", 0)],
                    writes=[("bigc", 3, "y", h)], tag="u")
            for h in range(H):
                wr = [("bigc", 1, "*")] if h == 0 else []
                p.op("dve", lambda e, h=h: e.tensor_scalar(
                    bigc[:, 2048 + h * 128:2048 + (h + 1) * 128],
                    bigc[:, 6144 + h * 128:6144 + (h + 1) * 128],
                    sa[:, S_WAL + h:S_WAL + h + 1], ZCOL, ALU.mult, ALU.add),
                    reads=[("bigc", 3, "y", h), ("sa", "wal"), ("cons", 0)],
                    writes=wr + [("bigc", 1, "upr", h)], tag="upr")

            for half in range(2):
                for j in range(8):
                    h = half * 8 + j
                    qsl = qfm[:, h * TILE + cl * 128:h * TILE + (cl + 1) * 128]
                    paq, pkq = pfb(5, 128, off=(j % 4) * 128)
                    p.op("pe", lambda e, paq=paq, qsl=qsl, h=h: e.matmul(
                        paq, qsl, sbf[:, h * DV:(h + 1) * DV],
                        start=True, stop=True),
                        reads=[("qfm", h, cl), ("sbf", h)], writes=pkq,
                        tag="qs0")
                    pao, pko = pfb(6, 128, off=(j % 4) * 128)
                    p.op("pe", lambda e, pao=pao, h=h: e.matmul(
                        pao, bigc[:, h * 128:(h + 1) * 128],
                        bigc[:, 6144 + h * 128:6144 + (h + 1) * 128],
                        start=True, stop=True),
                        reads=[("bigc", 0, "es", h), ("bigc", 3, "y", h)],
                        writes=pko, tag="o2")
                    p.op("dve", lambda e, paq=paq, h=h: e.tensor_scalar(
                        ms[:, M_T2:M_T2 + 128], paq,
                        sa[:, S_GAM + h:S_GAM + h + 1], ZCOL,
                        ALU.mult, ALU.add),
                        reads=pkq + [("sa", "gam"), ("cons", 0)],
                        writes=[("ms", "t2")], tag="t2")
                    p.op("dve", lambda e, pao=pao, j=j: e.tensor_tensor(
                        ostg[:, (j % 2) * 128:(j % 2) * 128 + 128],
                        ms[:, M_T2:M_T2 + 128], pao, ALU.add),
                        reads=pko + [("ms", "t2")], writes=[("ostg", j % 2)],
                        tag="ocmb")
                    pta, ptk = ptb(j * 128, 128)
                    p.op("pe", lambda e, pta=pta, j=j: e.transpose(
                        pta, ostg[:, (j % 2) * 128:(j % 2) * 128 + 128],
                        idbf[:]),
                        reads=[("ostg", j % 2), ("idbf", 0)], writes=ptk,
                        tag="oT")

                ptfence()

                def oev(e, half=half, cl=cl):
                    src = PT[:, 0:1024].rearrange("p (j f) -> p j f", j=8)
                    d = vfm[:].rearrange("p (h f) -> p h f", f=TILE)[
                        :, half * 8:(half + 1) * 8, cl * 128:(cl + 1) * 128]
                    return e.tensor_copy(d, src)
                p.op("dve", oev, reads=[("PT", 0)],
                     writes=[("vfm", half * 8 + j, cl) for j in range(8)],
                     tag="oev")

            for half in range(2):
                for j in range(8):
                    h = half * 8 + j
                    ksl = kfm[:, h * TILE + cl * 128:h * TILE + (cl + 1) * 128]
                    pta, ptk = ptb(j * 128, 128)
                    p.op("pe", lambda e, pta=pta, ksl=ksl: e.transpose(
                        pta, ksl, idbf[:]),
                        reads=[("kfm", h, cl), ("idbf", 0)], writes=ptk,
                        tag="kT")
                ptfence()
                wr = [("bigb", 3, "*")] if half == 0 else []
                p.op("act", lambda e, half=half: e.activation(
                    bigb[:, 6144 + half * 1024:6144 + (half + 1) * 1024],
                    PT[:, 0:1024], AF.Copy), reads=[("PT", 0)],
                    writes=wr + [("bigb", 3, "kt", half)], tag="ktev")

            for half in range(2):
                pa2, pk2 = pfb(4, 1024)
                for j in range(8):
                    h = half * 8 + j
                    p.op("pe", lambda e, h=h, j=j: e.matmul(
                        PF[:, 2048 + j * 128:2048 + (j + 1) * 128],
                        bigb[:, 6144 + h * 128:6144 + (h + 1) * 128],
                        bigc[:, 2048 + h * 128:2048 + (h + 1) * 128],
                        start=True, stop=True),
                        reads=[("bigb", 3, "kt", half), ("bigc", 1, "upr", h)],
                        writes=pk2, tag="stmm")
                for j in range(8):
                    h = half * 8 + j
                    p.op("dve", lambda e, h=h, j=j: e.scalar_tensor_tensor(
                        sall[:, h * DV:(h + 1) * DV],
                        sall[:, h * DV:(h + 1) * DV],
                        sa[:, S_GCC + h:S_GCC + h + 1],
                        PF[:, 2048 + j * 128:2048 + (j + 1) * 128],
                        ALU.mult, ALU.add),
                        reads=pk2 + [("sall", h), ("sa", "gcc")],
                        writes=[("sall", h)], tag="sup")
            p.op("dve", lambda e: e.tensor_copy(sbf[:], sall[:]),
                 reads=[("sall", h) for h in range(H)],
                 writes=[("sbf", h) for h in range(H)], tag="scast")

        def z_stage(b, ti):
            for hi in range(H + 1):
                if hi < H:
                    par = hi % 2
                    p.dma("qsp", lambda e, hi=hi, par=par: e.dma_start(
                        out=tmpb[:, par * 2048:(par + 1) * 2048],
                        in_=wz_s[hi]),
                        reads=[("tmpb", par), wreg("wz_s", hi, H)],
                        writes=[("tmpb", par)], tag="wzL")
                if hi == 0:
                    continue
                h = hi - 1
                par = h % 2
                wv = tmpb[:, par * 2048:(par + 1) * 2048]
                pa, pk = pfb(h % 2)
                for dc in range(NDC):
                    p.op("pe", lambda e, dc=dc, pa=pa, wv=wv: e.matmul(
                        pa, wv[:, dc * 128:(dc + 1) * 128],
                        xnfm[:, dc * TILE:(dc + 1) * TILE],
                        start=(dc == 0), stop=(dc == NDC - 1)),
                        reads=[("tmpb", par)] +
                        [("xnfm", dc, tg) for tg in range(4)],
                        writes=pk, tag="zmm")
                zpar = h % 2
                zsil = zs[:, 512 + zpar * 512:512 + (zpar + 1) * 512]
                p.op("act", lambda e, pa=pa, zsil=zsil: e.activation(
                    zsil, pa, AF.Silu),
                    reads=pk, writes=[("zs", "sil", zpar)], tag="zsil")
                osl = vfm[:, h * TILE:(h + 1) * TILE]
                okeys = [("vfm", h, cc) for cc in range(NCPT)]
                p.op("dve", lambda e, osl=osl, zsil=zsil: e.tensor_tensor(
                    ogf[:], osl, zsil, ALU.mult),
                    reads=okeys + [("zs", "sil", zpar)], writes=[("ogf", 0)],
                    tag="og")
                p.op("act", lambda e: e.activation(
                    zs[:, 1536:2048], ogf[:], AF.Square),
                    reads=[("ogf", 0)], writes=[("zs", "og2")], tag="og2")
                pa2, pk2 = pfb(2 + h % 2, 512)
                p.op("pe", lambda e, pa2=pa2: e.matmul(
                    pa2[0:1, :], onescol[:], zs[:, 1536:2048],
                    start=True, stop=True),
                    reads=[("zs", "og2"), ("onescol", 0)], writes=pk2,
                    tag="rmsmm")
                p.op("act", lambda e, pa2=pa2: e.activation(
                    rowbb[0:1, :], pa2[0:1, :], AF.Ln, scale=0.0078125,
                    bias=RMS_EPS),
                    reads=pk2, writes=[("rowbb", 0)], tag="rmsln")
                p.op("act", lambda e: e.activation(
                    rowbb[0:1, :], rowbb[0:1, :], AF.Exp, scale=-0.5),
                    reads=[("rowbb", 0)], writes=[("rowbb", 0)], tag="rmsex")
                pb2, pbk2 = pfb(2 + h % 2, 512)
                p.op("pe", lambda e, pb2=pb2: e.matmul(
                    pb2, onesrow[:], rowbb[0:1, :], start=True, stop=True),
                    reads=[("rowbb", 0), ("onesrow", 0)], writes=pbk2,
                    tag="rmsbc")
                p.op("dve", lambda e, pb2=pb2: e.scalar_tensor_tensor(
                    ogf[:], ogf[:], ms[:, M_NW:M_NW + 1], pb2,
                    ALU.mult, ALU.mult),
                    reads=[("ogf", 0), ("ms", "nw")] + pbk2,
                    writes=[("ogf", 0)], tag="onw")
                p.op("dve", lambda e, osl=osl: e.tensor_copy(osl, ogf[:]),
                     reads=[("ogf", 0)], writes=okeys, tag="oncp")

        def wout_stage(b, ti):
            t0 = ti * TILE
            for ogi in range(9):
                if ogi < 8:
                    par = ogi % 2
                    p.dma("qsp", lambda e, ogi=ogi, par=par: e.dma_start(
                        out=woutq[:, par * 4096:(par + 1) * 4096],
                        in_=wout_s[ogi]),
                        reads=[("woutq", par), wreg("wout_s", ogi, 8)],
                        writes=[("woutq", par)], tag="woutL")
                if ogi == 0:
                    continue
                og = ogi - 1
                par = og % 2
                wv = woutq[:, par * 4096:(par + 1) * 4096]
                for tg in range(4):
                    pa, pk = pfb(tg, 256)
                    for vc in range(H):
                        lhs = vfm[:, vc * TILE + tg * 128:
                                  vc * TILE + (tg + 1) * 128]
                        p.op("pe", lambda e, pa=pa, lhs=lhs, vc=vc, wv=wv:
                             e.matmul(pa, lhs,
                                      wv[:, vc * 256:(vc + 1) * 256],
                                      start=(vc == 0), stop=(vc == H - 1)),
                             reads=[("woutq", par), ("vfm", vc, tg)],
                             writes=pk, tag="womm")
                    wr = ([("bigb", tg, "*")] if og == 0 else [])
                    p.op("act", lambda e, pa=pa, tg=tg, og=og: e.activation(
                        bigb[:, tg * 2048 + og * 256:
                             tg * 2048 + (og + 1) * 256], pa, AF.Copy),
                        reads=pk, writes=wr + [("bigb", tg, "ao", og)],
                        tag="aoev")
            for tg in range(4):
                aosl = bigb[:, tg * 2048:(tg + 1) * 2048]
                aor = [("bigb", tg, "ao", og) for og in range(8)]
                ln_rstd(aosl, aor, "lna")
                p.dma("qact", lambda e, tg=tg: e.dma_start(
                    out=xstag[:],
                    in_=x_in[b, t0 + tg * 128:t0 + (tg + 1) * 128, :]),
                    reads=[("xstag", 0)], writes=[("xstag", 0)], tag="xre")
                p.op("dve", lambda e, aosl=aosl: e.tensor_scalar(
                    ytm[:], aosl, sa[:, S_LNM:S_LNM + 1],
                    sa[:, S_LNR:S_LNR + 1], ALU.subtract, ALU.mult),
                    reads=aor + [("sa", "mv"), ("sa", "r")],
                    writes=[("ytm", 0)], tag="lnaap")
                p.op("dve", lambda e: e.tensor_tensor(
                    ytm[:], ytm[:], xstag[:], ALU.add),
                    reads=[("ytm", 0), ("xstag", 0)], writes=[("ytm", 0)],
                    tag="resid1")
                p.dma("qact", lambda e, tg=tg, b=b, t0=t0: e.dma_start(
                    out=x2_s[b, t0 + tg * 128:t0 + (tg + 1) * 128, :],
                    in_=ytm[:]),
                    reads=[("ytm", 0)], writes=[("x2s", b, ti * 4 + tg)],
                    tag="x2w")

        def ffn_block(fb):
            b = fb // NTPB
            t0 = (fb % NTPB) * 512
            for tg in range(4):
                p.dma("qact", lambda e, tg=tg, b=b, t0=t0: e.dma_start(
                    out=xstag[:],
                    in_=x2_s[b, t0 + tg * 128:t0 + (tg + 1) * 128, :]),
                    reads=[("xstag", 0), ("x2s", b, (t0 // 128) + tg)],
                    writes=[("xstag", 0)], tag="x2r")
                ln_rstd(xstag, [("xstag", 0)], "ln2")
                p.op("dve", lambda e: e.tensor_scalar(
                    xntm[:], xstag[:], sa[:, S_LNM:S_LNM + 1],
                    sa[:, S_LNR:S_LNR + 1], ALU.subtract, ALU.mult),
                    reads=[("xstag", 0), ("sa", "mv"), ("sa", "r")],
                    writes=[("xntm", 0)], tag="ln2ap")
                extra = [("vfm", "*")] if (tg == 0 and fb == 0) else None
                transpose_16_to(vfm, False, tg, extra, "xn2", "xn2T")
            xn2 = lambda dc: [("xn2", dc, tg) for tg in range(4)]
            if fb == 0:
                dbg("ffn_xn2", lambda: vfm[:],
                    [("xn2", dc, tg) for dc in range(16) for tg in range(4)],
                    [128, 8192], BF16)

            slot_t = [qfm, qfm, kfm, kfm]
            slot_o = [0, 4096, 0, 4096]
            slot_k = [("qfm", "hf", 0), ("qfm", "hf", 1),
                      ("kfm", "hf", 0), ("kfm", "hf", 1)]

            def slot_ap(si):
                return slot_t[si][:, slot_o[si]:slot_o[si] + 4096]

            isl_order = [q + 4 * m for q in range(4) for m in range(4)]
            subs = []
            for oi, isl in enumerate(isl_order):
                for u in range(4):
                    subs.append((isl, u, (oi * 4 + u) % 4))

            def sub_dma(k, fb=fb):
                isl, u, si = subs[k]
                wr = [slot_k[si]]
                if fb == 0 and k < 4:
                    wr = [(slot_k[si][0], "*")] + wr
                p.dma("qsp", lambda e, isl=isl, u=u, si=si: e.dma_start(
                    out=slot_ap(si),
                    in_=bass.AP(wgu_s, isl * 128 * 16384 + u * 4096,
                                [[16384, 128], [1, 4096]])),
                    reads=[slot_k[si], wreg("wgu_s", isl, NIS)],
                    writes=wr, tag="wguL")

            NPRE = 3
            for k in range(NPRE):
                sub_dma(k)
            subi = NPRE
            for oi, isl in enumerate(isl_order):
                q, m = isl % 4, isl // 4
                base = oi * 4
                gslots = [subs[base][2], subs[base + 1][2]]
                uslots = [subs[base + 2][2], subs[base + 3][2]]
                for tg in range(4):
                    pa, pk = pfb(tg)
                    for dc in range(NDC):
                        si = gslots[dc // 8]
                        sap = slot_ap(si)
                        p.op("pe", lambda e, pa=pa, sap=sap, dc=dc, tg=tg, si=si:
                             e.matmul(
                                 pa, vfm[:, dc * 512 + tg * 128:
                                         dc * 512 + (tg + 1) * 128],
                                 sap[:, (dc % 8) * 512:(dc % 8 + 1) * 512],
                                 start=(dc == 0), stop=(dc == NDC - 1)),
                             reads=[slot_k[si], ("xn2", dc, tg)],
                             writes=pk, tag="gmm")
                    zclaim = ([("zs", "*")] if (fb == 0 and oi == 0 and
                                                tg == 0) else [])
                    p.op("act", lambda e, pa=pa, tg=tg: e.activation(
                        zs[:, tg * 512:(tg + 1) * 512], pa, AF.Silu),
                        reads=pk, writes=zclaim + [("zs", "hg", tg)],
                        tag="hgsilu")
                if subi < len(subs):
                    sub_dma(subi)
                    subi += 1
                for tg in range(4):
                    pa, pk = pfb(tg)
                    for dc in range(NDC):
                        si = uslots[dc // 8]
                        sap = slot_ap(si)
                        p.op("pe", lambda e, pa=pa, sap=sap, dc=dc, tg=tg, si=si:
                             e.matmul(
                                 pa, vfm[:, dc * 512 + tg * 128:
                                         dc * 512 + (tg + 1) * 128],
                                 sap[:, (dc % 8) * 512:(dc % 8 + 1) * 512],
                                 start=(dc == 0), stop=(dc == NDC - 1)),
                             reads=[slot_k[si], ("xn2", dc, tg)],
                             writes=pk, tag="umm")
                    hslice = bigb[:, tg * 2048 + m * 512:
                                  tg * 2048 + (m + 1) * 512]
                    hkey = ("bigb", tg, "h", m)
                    wr = [("bigb", tg, "*")] if (m == 0 and q == 0) else []
                    p.op("dve", lambda e, pa=pa, hslice=hslice, tg=tg:
                         e.tensor_tensor(hslice,
                                         zs[:, tg * 512:(tg + 1) * 512],
                                         pa, ALU.mult),
                         reads=pk + [("zs", "hg", tg)], writes=wr + [hkey],
                         tag="hmul")
                    sqsl = woutq[:, tg * 2048 + m * 512:
                                 tg * 2048 + (m + 1) * 512]
                    wr2 = []
                    if fb == 0 and m == 0 and q == 0 and tg == 0:
                        wr2 = [("woutq", "*")]
                    p.op("act", lambda e, hslice=hslice, sqsl=sqsl, hkey=hkey:
                         e.activation(sqsl, hslice, AF.Square),
                         reads=[hkey],
                         writes=wr2 + [("woutq", "q", tg, "sq", m)],
                         tag="hsq")
                for _ in range(3):
                    if subi < len(subs):
                        sub_dma(subi)
                        subi += 1
                if m == 3:
                    if fb == 0 and q == 0:
                        dbg("ffn_h_q0", lambda: bigb[:],
                            [("bigb", tg, "h", mm) for tg in range(4)
                             for mm in range(4)], [128, 8192], BF16)
                        dbg("ffn_sq_q0", lambda: woutq[:],
                            [("woutq", "q", tg, "sq", mm) for tg in range(4)
                             for mm in range(4)], [128, 8192], BF16)
                    for tg in range(4):
                        def sq(mm, tg=tg):
                            return woutq[:, tg * 2048 + mm * 512:
                                         tg * 2048 + (mm + 1) * 512]
                        sqr = [("woutq", "q", tg, "sq", mm) for mm in range(4)]
                        TA = tmpb[:, 4096:4608]
                        TB = tmpb[:, 4608:5120]
                        TC = tmpb[:, 5120:5632]
                        p.op("dve", lambda e, sq=sq: e.tensor_tensor(
                            TA, sq(0), sq(1), ALU.max), reads=sqr,
                            writes=[("tmpb", "ta")], tag="spm1")
                        p.op("dve", lambda e, sq=sq: e.tensor_tensor(
                            TB, sq(2), sq(3), ALU.max), reads=sqr,
                            writes=[("tmpb", "tb")], tag="spm2")
                        p.op("dve", lambda e: e.tensor_tensor(
                            TA, TA, TB, ALU.min),
                            reads=[("tmpb", "ta"), ("tmpb", "tb")],
                            writes=[("tmpb", "ta")], tag="splo")
                        p.op("dve", lambda e, sq=sq: e.tensor_tensor(
                            TC, sq(0), sq(1), ALU.min), reads=sqr,
                            writes=[("tmpb", "tc")], tag="spn1")
                        p.op("dve", lambda e, sq=sq: e.tensor_tensor(
                            TB, sq(2), sq(3), ALU.min), reads=sqr,
                            writes=[("tmpb", "tb")], tag="spn2")
                        p.op("dve", lambda e: e.tensor_tensor(
                            TB, TB, TC, ALU.max),
                            reads=[("tmpb", "tb"), ("tmpb", "tc")],
                            writes=[("tmpb", "tb")], tag="sphi")
                        p.op("dve", lambda e: e.tensor_tensor(
                            TA, TA, TB, ALU.max),
                            reads=[("tmpb", "ta"), ("tmpb", "tb")],
                            writes=[("tmpb", "ta")], tag="sptau")
                        for mm in range(4):
                            p.op("dve", lambda e, sq=sq, mm=mm: e.tensor_tensor(
                                TB, sq(mm), TA, ALU.is_ge),
                                reads=sqr + [("tmpb", "ta")],
                                writes=[("tmpb", "tb")], tag="spge")
                            hs = bigb[:, tg * 2048 + mm * 512:
                                      tg * 2048 + (mm + 1) * 512]
                            p.op("dve", lambda e, hs=hs: e.tensor_tensor(
                                hs, hs, TB, ALU.mult),
                                reads=[("bigb", tg, "h", mm), ("tmpb", "tb")],
                                writes=[("bigb", tg, "h", mm)], tag="spmul")
                    if fb == 0 and q == 0:
                        dbg("ffn_hs_q0", lambda: bigb[:],
                            [("bigb", tg, "h", mm) for tg in range(4)
                             for mm in range(4)], [128, 8192], BF16)
                    trs = [(tg, mm, jj) for mm in range(4) for jj in range(4)
                           for tg in range(4)]
                    for r0 in range(0, len(trs), 8):
                        grp = trs[r0:r0 + 8]
                        for gi, (tg, mm, jj) in enumerate(grp):
                            pta, ptk = ptb(gi * 128, 128)
                            src = bigb[:, tg * 2048 + mm * 512 + jj * 128:
                                       tg * 2048 + mm * 512 + (jj + 1) * 128]
                            p.op("pe", lambda e, pta=pta, src=src: e.transpose(
                                pta, src, idbf[:]),
                                reads=[("bigb", tg, "h", mm), ("idbf", 0)],
                                writes=ptk, tag="hT")
                        ptfence()
                        for gi, (tg, mm, jj) in enumerate(grp):
                            pta, ptk = ptb(gi * 128, 128)
                            col = (mm * 4 + jj) * 512 + tg * 128
                            wrx = []
                            p.op("act", lambda e, pta=pta, col=col:
                                 e.activation(bigc[:, col:col + 128], pta,
                                              AF.Copy),
                                 reads=[("PT", 0)],
                                 writes=[("bigc", "stg", (mm * 4 + jj), tg)],
                                 tag="hTev")

                    for mm in range(4):
                        def hout(e, q=q, mm=mm):
                            src = bigc[:, mm * 2048:(mm + 1) * 2048].rearrange(
                                "p (jj f) -> p jj f", jj=4)
                            dst = bass.AP(
                                hfm_s, (4 * q + 16 * mm) * 65536,
                                [[512, 128], [65536, 4], [1, 512]])
                            return e.dma_start(out=dst, in_=src)
                        p.dma("qact", hout,
                              reads=[("bigc", "stg", mm * 4 + jj, tg)
                                     for jj in range(4) for tg in range(4)],
                              writes=[("hfmd", q, mm)], tag="hout")

            if fb == 0 and DEBUG:
                t_hfm = nc.dram_tensor("dbg_hfm", [64, 128, 512], BF16,
                                       kind="ExternalOutput")
                p.dma("qact", lambda e: e.dma_start(out=t_hfm.ap(),
                                                    in_=hfm_s.ap()),
                      reads=[("hfmd", qq, mm) for qq in range(4)
                             for mm in range(4)],
                      writes=[("dbgw", "hfm")], tag="dbg")
            dslots = [("qfm", "hf", 0), ("qfm", "hf", 1), ("kfm", "hf", 0),
                      ("kfm", "hf", 1), ("wbuf", "hf", 0), ("wbuf", "hf", 1)]
            dslot_t = [qfm, qfm, kfm, kfm, wbuf, wbuf]
            dslot_o = [0, 4096, 0, 4096, 0, 4096]

            def dslot_ap(si):
                return dslot_t[si][:, dslot_o[si]:dslot_o[si] + 4096]

            units = []
            for osx in range(4):
                for ig in range(8):
                    units.append(("h", osx, ig))
                    units.append(("w", osx, ig))
            uslot = {}

            def unit_dma(k, fb=fb):
                kind = units[k]
                si = k % 6
                uslot[k] = si
                wr = [dslots[si]]
                if fb == 0 and kind[0] == "w" and kind[1] == 0 and kind[2] == 0:
                    wr = [("wbuf", "*")] + wr
                if kind[0] == "h":
                    _, osx, ig = kind
                    def f(e, ig=ig, si=si):
                        src = bass.AP(hfm_s, ig * 8 * 65536,
                                      [[512, 128], [65536, 8], [1, 512]])
                        return e.dma_start(out=dslot_ap(si), in_=src)
                    rd = [dslots[si]] + [("hfmd", qq, mm) for qq in range(4) for mm in range(4)]
                else:
                    _, osx, ig = kind
                    def f(e, osx=osx, ig=ig, si=si):
                        return e.dma_start(out=dslot_ap(si),
                                           in_=wd_s[osx, ig])
                    rd = [dslots[si], wreg("wd_s", osx * 8 + ig, 32)]
                p.dma("qsp", f, reads=rd, writes=wr, tag="d" + kind[0])

            for k in range(4):
                unit_dma(k)
            uk = 4
            for step in range(32):
                osx, g = step // 8, step % 8
                bidx = step * 2
                sh = uslot[bidx]
                sw = uslot[bidx + 1]
                hap = dslot_ap(sh)
                wv = dslot_ap(sw)
                for tg in range(4):
                    pa, pk = pfb(tg)
                    for jj in range(8):
                        p.op("pe", lambda e, pa=pa, hap=hap, wv=wv,
                             jj=jj, tg=tg, g=g: e.matmul(
                                 pa, hap[:, jj * 512 + tg * 128:
                                         jj * 512 + (tg + 1) * 128],
                                 wv[:, jj * 512:(jj + 1) * 512],
                                 start=(g == 0 and jj == 0),
                                 stop=(g == 7 and jj == 7)),
                             reads=[dslots[sh], dslots[sw]], writes=pk,
                             tag="dmm")
                    if g == 7:
                        wrx = []
                        if fb == 0 and tg == 0 and osx == 0:
                            wrx = [("xnfm", "*")]
                        p.op("act", lambda e, pa=pa, tg=tg, osx=osx:
                             e.activation(
                                 xnfm[:, tg * 2048 + osx * 512:
                                      tg * 2048 + (osx + 1) * 512],
                                 pa, AF.Copy),
                             reads=pk,
                             writes=wrx + [("xnfm", "y", tg, osx)],
                             tag="yev")
                while uk < min(len(units), (step + 2) * 2 + 2):
                    unit_dma(uk)
                    uk += 1
            if fb == 0:
                dbg("ffn_y", lambda: xnfm[:],
                    [("xnfm", "y", tg, osx) for tg in range(4)
                     for osx in range(4)], [128, 8192], BF16)
            for tg in range(4):
                ysl = xnfm[:, tg * 2048:(tg + 1) * 2048]
                yr = [("xnfm", "y", tg, osx) for osx in range(4)]
                ln_rstd(ysl, yr, "lnf")
                p.dma("qact", lambda e, tg=tg, b=b, t0=t0: e.dma_start(
                    out=xstag[:],
                    in_=x2_s[b, t0 + tg * 128:t0 + (tg + 1) * 128, :]),
                    reads=[("xstag", 0), ("x2s", b, (t0 // 128) + tg)],
                    writes=[("xstag", 0)], tag="x2re")
                p.op("dve", lambda e, ysl=ysl: e.tensor_scalar(
                    ytm[:], ysl, sa[:, S_LNM:S_LNM + 1],
                    sa[:, S_LNR:S_LNR + 1], ALU.subtract, ALU.mult),
                    reads=yr + [("sa", "mv"), ("sa", "r")],
                    writes=[("ytm", 0)], tag="lnfap")
                p.op("dve", lambda e: e.tensor_tensor(
                    ytm[:], ytm[:], xstag[:], ALU.add),
                    reads=[("ytm", 0), ("xstag", 0)], writes=[("ytm", 0)],
                    tag="resid2")
                p.dma("qact", lambda e, tg=tg, b=b, t0=t0: e.dma_start(
                    out=out[b, t0 + tg * 128:t0 + (tg + 1) * 128, :],
                    in_=ytm[:]),
                    reads=[("ytm", 0)], writes=[("outd", fb, tg)], tag="outw")

        for b in range(B):
            p.op("dve", lambda e: e.memset(sall[:], 0.0),
                 writes=[("sall", h) for h in range(H)], tag="szero",
                 fence=True)
            p.op("dve", lambda e: e.memset(sbf[:], 0.0),
                 writes=[("sbf", h) for h in range(H)], tag="sbzero",
                 fence=True)
            p.op("dve", lambda e: e.memset(tails[:], 0.0),
                 writes=[("tails", q) for q in range(NQC)], tag="tzero",
                 fence=True)
            for ti in range(NTPB):
                attn_tile(b, ti)
        for fb in range(B * NTPB):
            ffn_block(fb)
        p.op("sp", lambda e: e.nop(),
             reads=[("outd", fb, tg) for fb in range(B * NTPB)
                    for tg in range(4)], tag="fin")
        if DEBUG:
            p.op("pool", lambda e: e.engine_nop(),
                 reads=[("outd", fb, tg) for fb in range(B * NTPB)
                        for tg in range(4)],
                 writes=[("alldone",)], tag="dbgfence")
            dbgdram("x2", x2_s, [B, T, D], F32)
            dbgdram("gflat", gflat_s, [B, NTPB, NCPT, H, C], F32)

        p.finalize(block, sems)
    return nc



def _prep_weights(W_qkv, conv_w, W_z, W_b, W_a, dt_bias, A_log, norm_weight,
                  W_out, w_gate, w_up, w_down):
    f = np.float32
    wqkv = np.zeros((NQC, 128, 2560), f)
    wqkv[:, :, :2048] = W_qkv.reshape(16, 128, 48, 128).transpose(
        2, 1, 0, 3).reshape(48, 128, 2048)
    cw = conv_w[:, 0, :].reshape(48, 128, KCV)
    dg = np.zeros((48, KCV, 128, 128), f)
    idx = np.arange(128)
    dg[:, :, idx, idx] = cw.transpose(0, 2, 1)
    wqkv[:, :, 2048:] = dg.transpose(0, 2, 1, 3).reshape(48, 128, 512)

    wba_pad = np.zeros((D, 64), f)
    wba_pad[:, 0:16] = W_b
    wba_pad[:, 32:48] = W_a
    wba = wba_pad.reshape(16, 128, 64).transpose(1, 0, 2).reshape(128, 1024)

    wz = W_z.reshape(16, 128, 16, 128).transpose(2, 1, 0, 3).reshape(
        16, 128, 2048)
    wout = W_out.reshape(16, 128, 8, 256).transpose(2, 1, 0, 3).reshape(
        8, 128, 4096)

    perm = np.arange(II).reshape(II // 4, 4).T.reshape(-1)
    wgp = np.ascontiguousarray(w_gate[:, perm])
    wup = np.ascontiguousarray(w_up[:, perm])
    wdp = np.ascontiguousarray(w_down[perm, :])
    g_blk = wgp.reshape(16, 128, 16, 512).transpose(2, 1, 0, 3).reshape(
        16, 128, 8192)
    u_blk = wup.reshape(16, 128, 16, 512).transpose(2, 1, 0, 3).reshape(
        16, 128, 8192)
    wgu = np.concatenate([g_blk, u_blk], axis=2)

    wd = wdp.reshape(64, 128, 4, 512).transpose(2, 0, 1, 3)
    wd = wd.reshape(4, 8, 8, 128, 512).transpose(0, 1, 3, 2, 4).reshape(
        4, 8, 128, 4096)

    scal = np.zeros((64, 8), f)
    scal[0:16, 0] = A_log
    scal[32:48, 0] = dt_bias

    pen = np.where(np.arange(128)[:, None] > np.arange(128)[None, :],
                   np.float32(NEG_BIG), np.float32(0.0)).astype(f)
    ident = np.eye(128, dtype=f)

    cvals = np.zeros((128, 16), f)
    for i, v in enumerate(CONSTS_LIST):
        cvals[:, i] = np.float32(v)

    return dict(
        wqkv_blk=np.ascontiguousarray(wqkv),
        wba_blk=np.ascontiguousarray(wba),
        wz_blk=np.ascontiguousarray(wz),
        wout_blk=np.ascontiguousarray(wout),
        wgu_blk=np.ascontiguousarray(wgu),
        wd_blk=np.ascontiguousarray(wd),
        scal_in=scal,
        idf_in=ident,
        idb_in=ident.astype(ml_dtypes.bfloat16),
        penb_in=pen.astype(ml_dtypes.bfloat16),
        nw_in=np.ascontiguousarray(norm_weight.reshape(128, 1).astype(f)),
        cns=cvals,
    )


def kernel(L0_attn_hidden_states, L0_attn_W_qkv, L0_attn_conv_w, L0_attn_W_z,
           L0_attn_W_b, L0_attn_W_a, L0_attn_dt_bias, L0_attn_A_log,
           L0_attn_norm_weight, L0_attn_W_out, L0_ffn_w_gate, L0_ffn_w_up,
           L0_ffn_w_down):
    x = np.ascontiguousarray(np.asarray(L0_attn_hidden_states, np.float32))
    wmap = _prep_weights(
        np.asarray(L0_attn_W_qkv, np.float32),
        np.asarray(L0_attn_conv_w, np.float32),
        np.asarray(L0_attn_W_z, np.float32),
        np.asarray(L0_attn_W_b, np.float32),
        np.asarray(L0_attn_W_a, np.float32),
        np.asarray(L0_attn_dt_bias, np.float32),
        np.asarray(L0_attn_A_log, np.float32),
        np.asarray(L0_attn_norm_weight, np.float32),
        np.asarray(L0_attn_W_out, np.float32),
        np.asarray(L0_ffn_w_gate, np.float32),
        np.asarray(L0_ffn_w_up, np.float32),
        np.asarray(L0_ffn_w_down, np.float32))

    nc = build_nc()
    in_maps = []
    for core in range(NCORES):
        m = dict(wmap)
        m["x"] = np.ascontiguousarray(x[core * B:(core + 1) * B])
        in_maps.append(m)
    res = run_bass_kernel_spmd(nc, in_maps, core_ids=list(range(NCORES)))
    return np.concatenate([r["out"] for r in res.results], axis=0)
